# version 06 (final)
# baseline (speedup 1.0000x reference)
import numpy as np
import ml_dtypes
import concourse.bass as bass
import concourse.mybir as mybir
from concourse.bass_utils import run_bass_kernel_spmd

F32 = mybir.dt.float32
BF16 = mybir.dt.bfloat16
U32 = mybir.dt.uint32
I32 = mybir.dt.int32
AF = mybir.ActivationFunctionType
ALU = mybir.AluOpType
AX = mybir.AxisListType

MAGIC = 1.5 * 2.0 ** 23
SCALE_QK = 1.0 / (128.0 ** 0.25)
B, S, D = 4, 2048, 2048
HL = 8
NK = 16
TOWN = 1024
NTT = 8
H0, H1 = 8192, 5120
EPS = 1e-5
RG8 = [[0, 1, 2, 3, 4, 5, 6, 7]]
ENGS = ("sync", "scalar", "vector", "tensor", "gpsimd")


class KB:
    def __init__(self, nc):
        self.nc = nc
        self.ops = {e: [] for e in ENGS}
        self.cnt = {}
        self.lastwait = {e: {} for e in ENGS}
        self.sems = {}

    def emit(self, eng, fn, waits=(), sig=None, inc=1):
        w = []
        for tok in waits:
            if tok is None:
                continue
            s, v = tok
            if v > 0 and self.lastwait[eng].get(s, 0) < v:
                w.append((s, v))
                self.lastwait[eng][s] = v
        tok = None
        if sig:
            self.cnt[sig] = self.cnt.get(sig, 0) + inc
            tok = (sig, self.cnt[sig])

        def op(e, _w=tuple(w), _fn=fn, _sig=sig, _inc=inc):
            for s, v in _w:
                e.wait_ge(self.sems[s], v)
            ins = _fn(e)
            if _sig is not None:
                ins.then_inc(self.sems[_sig], _inc)

        self.ops[eng].append(op)
        return tok

    def dma(self, q, out, in_, waits=(), **kw):
        sem = {"sync": "qs", "scalar": "qa", "gpsimd": "qg"}[q]
        return self.emit(q, lambda e, o=out, i=in_, kk=dict(kw): e.dma_start(out=o, in_=i, **kk),
                         waits=waits, sig=sem, inc=16)

    def tens(self, fn, waits=(), sig=True):
        return self.emit("tensor", fn, waits, "st" if sig else None)

    def vec(self, fn, waits=(), sig=True):
        return self.emit("vector", fn, waits, "sv" if sig else None)

    def scal(self, fn, waits=(), sig=True):
        return self.emit("scalar", fn, waits, "ss" if sig else None)

    def gps(self, fn, waits=(), sig=True):
        return self.emit("gpsimd", fn, waits, "sg" if sig else None)

    def tok(self, sem):
        return (sem, self.cnt.get(sem, 0))

    def barrier(self):
        toks = [self.tok(s) for s in ("qs", "qa", "qg", "sv", "ss", "st", "sg")]
        for e in ENGS:
            self.emit(e, lambda ee: ee.nop(), waits=toks)


def build(debug_taps=False, stop_after=None):
    import contextlib
    nc = bass.Bass("TRN2", target_bir_lowering=False)
    stack = contextlib.ExitStack()

    def inp(name, shape, dt=F32):
        return nc.dram_tensor(name, shape, dt, kind="ExternalInput").ap()

    x_own = inp("x_own", [TOWN, D])
    x_seq = inp("x_seq", [S, D])
    wq_s = inp("wq_s", [HL * 128, D])
    wk_s = inp("wk_s", [HL * 128, D])
    wv_s = inp("wv_s", [HL * 128, D])
    wo_full = inp("wo_full", [D, D])
    bq_s = inp("bq_s", [HL * 128])
    bk_s = inp("bk_s", [HL * 128])
    bv_s = inp("bv_s", [HL * 128])
    bo_full = inp("bo_full", [D])
    gate_w_s = inp("gate_w_s", [H0 // 8, D])
    val_w_s = inp("val_w_s", [H0 // 8, D])
    out_w_s = inp("out_w_s", [D, H0 // 8])
    gate_b = inp("gate_b", [H0])
    val_b = inp("val_b", [H0])
    out_b = inp("out_b", [D])
    wqkv_s = inp("wqkv_s", [D, 3 * HL * 128])
    wout1 = inp("wout1", [D, D])
    decay_s = inp("decay_s", [1, HL])
    w_fc_s = inp("w_fc_s", [H1 // 8, D])
    w_proj_s = inp("w_proj_s", [D, H1 // 8])
    smask_in = inp("smask_in", [128, 896], BF16)
    gidx_in = inp("gidx_in", [128, 48], U32)

    y_out = nc.dram_tensor("y", [TOWN, D], F32, kind="ExternalOutput").ap()
    taps = {}
    if debug_taps:
        for t in ("a0", "f0", "a1"):
            taps[t] = nc.dram_tensor("tap_" + t, [TOWN, D], F32, kind="ExternalOutput").ap()
        taps["qf"] = nc.dram_tensor("tap_qf", [HL * 128, S], BF16, kind="ExternalOutput").ap()
        taps["kf"] = nc.dram_tensor("tap_kf", [HL * 128, S], BF16, kind="ExternalOutput").ap()
        taps["vt"] = nc.dram_tensor("tap_vt", [HL, NK, 128, 128], BF16, kind="ExternalOutput").ap()
        taps["ot0"] = nc.dram_tensor("tap_ot0", [HL * 128 * 2, 1024], BF16, kind="ExternalOutput").ap()
        taps["xt"] = nc.dram_tensor("tap_xt", [128, 16, NK, 128], BF16, kind="ExternalOutput").ap()
        taps["otg"] = nc.dram_tensor("tap_otg", [128, 16, 1024], BF16, kind="ExternalOutput").ap()
        taps["st"] = nc.dram_tensor("tap_st", [128, 128], F32, kind="ExternalOutput").ap()
        taps["lnx"] = nc.dram_tensor("tap_lnx", [128, 2048], F32, kind="ExternalOutput").ap()

    def scr(name, shape, dt=BF16, **kw):
        return nc.dram_tensor(name, shape, dt, kind="Internal", **kw).ap()

    wqkvT_scr = scr("wqkvT_scr", [3, NK, HL, 128, 128])
    woT_scr = scr("woT_scr", [NK, NK, 128, 128])
    qf_scr = scr("qf_scr", [HL * 128, S])
    kf_scr = scr("kf_scr", [HL * 128, S])
    vt_scr = scr("vt_scr", [HL, NK, 128, 128])
    ag1_in = scr("ag1_in", [384, 128, 128])
    ag1_out = scr("ag1_out", [8 * 384, 128, 128], addr_space="Local")
    ag2_in = scr("ag2_in", [160, 128, 128])
    ag2_out = scr("ag2_out", [8 * 160, 128, 128], addr_space="Local")
    ar_in = scr("ar_in", [1, 8], F32)
    ar_out = scr("ar_out", [1, 8], F32, addr_space="Local")
    wqkv_bf = scr("wqkv_bf", [D, 3 * HL * 128])
    wout1_bf = scr("wout1_bf", [D, D])
    gv_scr = scr("gv_scr", [NTT, 128, H0])
    agot0_in = scr("agot0_in", [HL * 128 * 2, 1024])
    agot0_out = scr("agot0_out", [8 * HL * 128 * 2, 1024], addr_space="Local")
    agx_in = scr("agx_in", [NK * 128, 1024])
    agx_out = scr("agx_out", [8 * NK * 128, 1024], addr_space="Local")
    agot1_in = scr("agot1_in", [HL * 128 * 2, 1024])
    agot1_out = scr("agot1_out", [8 * HL * 128 * 2, 1024], addr_space="Local")

    def sb(name, shape, dt):
        return stack.enter_context(nc.sbuf_tensor(name, shape, dt))

    res = sb("res", [128, NTT, D], F32)
    xt = sb("xt", [128, 16, NK, 128], BF16)
    pool2 = sb("pool2", [128, 16384], BF16)
    hb = sb("hb", [128, 12288], BF16)
    wst = sb("wst", [128, 2, 16, 128], BF16)
    stage = sb("stage", [128, 2, 512], BF16)
    pt = sb("pt", [128, 2, 512], BF16)
    sumacc = sb("sumacc", [128, 512], F32)
    denr = sb("denr", [128, 512], F32)
    gbb = sb("gbb", [128, 1, 2, 512], F32)
    stats = sb("stats", [128, 128], F32)
    onesf = sb("onesf", [128, 128], F32)
    onesb = sb("onesb", [1, 128], BF16)
    ident = sb("ident", [128, 128], BF16)
    smask = sb("smask", [128, 896], BF16)
    gidx = sb("gidx", [128, 48], U32)
    gamb = sb("gamb", [128, 16], F32)
    decb = sb("decb", [128, HL], F32)

    pb = [stack.enter_context(nc.psum_tensor(f"pb{i}", [128, 512], F32)) for i in range(6)]
    ptr = stack.enter_context(nc.psum_tensor("ptr", [128, 1024], BF16))
    pden = stack.enter_context(nc.psum_tensor("pden", [128, 512], F32))

    hbf32 = hb[:, :].bitcast(F32)
    p2f32 = pool2[:, :].bitcast(F32)
    xtflat = xt[:, :, :, :].rearrange("p a k x -> p (a k x)")

    k = KB(nc)
    for s_ in ("qs", "qa", "qg", "sv", "ss", "st", "sg", "scc"):
        k.sems[s_] = stack.enter_context(nc.semaphore(s_))

    bank_tok = [None] * 6
    pden_tok = [None]
    ptr_tok = [None]
    stage_tok = [None, None]
    wst_tok = [None, None]

    k.dma("sync", smask[:], smask_in)
    t_gidxl = k.dma("sync", gidx[:], gidx_in)
    for i in range(NTT):
        k.dma("sync", res[:, i, :], x_own[i * 128:(i + 1) * 128, :])
    t_ones = k.vec(lambda e: e.memset(onesf[:], 1.0))
    t_onesb = k.vec(lambda e: e.memset(onesb[:], 1.0))
    t_eps = k.vec(lambda e: e.memset(stats[:, 127:128], EPS))
    iot = denr[:, 0:128].bitcast(I32)
    t_iot = k.gps(lambda e: e.iota(iot, [[1, 128]], base=0, channel_multiplier=-1))
    t_id = k.vec(lambda e: e.tensor_scalar(ident[:], iot, 0.0, None, ALU.is_equal),
                 waits=[t_iot])
    t_dec0 = k.dma("sync", stats[0:1, 96:96 + HL], decay_s)
    t_dec1 = k.tens(lambda e: e.matmul(pden[:, 0:HL], onesf[0:1, :], stats[0:1, 96:96 + HL],
                                       start=True, stop=True), waits=[t_dec0, t_ones])
    t_dec2 = k.scal(lambda e: e.activation(decb[:], pden[:, 0:HL], AF.Sigmoid), waits=[t_dec1])
    pden_tok[0] = t_dec2

    prep_mats = [("gate", gate_w_s, 8, 2048), ("val", val_w_s, 8, 2048),
                 ("out", out_w_s, 16, 1024), ("fc", w_fc_s, 5, 2048),
                 ("proj", w_proj_s, 16, 640)]
    counts = {"gate": H0 * D, "val": H0 * D, "out": H0 * D, "fc": H1 * D, "proj": H1 * D}
    t_z = k.vec(lambda e: e.memset(stats[:, 0:9], 0.0))
    st1 = {"nb": 0, "last": [None, None], "tsum": None, "gamb": None}
    tb_ln = [p2f32[:, 0:2048], p2f32[:, 2048:4096]]
    tst_ln = hb[:, 6144:7168]
    xtf32 = xtflat.bitcast(F32)
    tb_at = [xtf32[:, 0:2048], xtf32[:, 2048:4096]]
    qtmp_at = xtf32[:, 4096:6144]
    qbf_at = xtflat[:, 12288:14336]
    tst_at = xtflat[:, 16384:17408]

    def local_T_tile(src, ti, dest_rowfn, tbufs_, tstage_, first=False):
        bsel = st1["nb"] % 2
        st1["nb"] += 1
        tbf = tbufs_[bsel][:, 0:1024].bitcast(BF16)
        tld = k.dma("gpsimd", tbf, src[ti * 128:(ti + 1) * 128, :],
                    waits=[st1["last"][bsel], k.tok("st") if first else None])
        for g0 in (0, 8):
            ttr = None
            for j in range(8):
                ttr = k.tens(lambda e, t=tbf, g=g0, j=j: e.transpose(
                    ptr[:, j * 128:(j + 1) * 128], t[:, (g + j) * 128:(g + j + 1) * 128],
                    ident[:]), waits=[tld, t_id, ptr_tok[0] if j == 0 else None,
                                      k.tok("qa") if j == 0 else None],
                    sig=(j == 7))
            tev = k.vec(lambda e, ts=tstage_: e.tensor_copy(ts[:, 0:1024], ptr[:, 0:1024]),
                        waits=[ttr])
            ptr_tok[0] = tev
            for j in range(8):
                k.dma("scalar", dest_rowfn(g0 + j, ti), tstage_[:, j * 128:(j + 1) * 128],
                      waits=[tev])
        st1["last"][bsel] = k.tok("st")

    def pass1_tile(ap_, ti, w_, mi, tbufs_):
        bsel = st1["nb"] % 2
        st1["nb"] += 1
        tld = k.dma("gpsimd", tbufs_[bsel][:, 0:w_], ap_[ti * 128:(ti + 1) * 128, :],
                    waits=[st1["last"][bsel]])
        t_red = k.vec(lambda e, b=tbufs_[bsel], w=w_: e.tensor_reduce(
            stats[:, 8:9], b[:, 0:w], AX.X, ALU.add, apply_absolute_value=True),
            waits=[tld, t_z, st1["tsum"]])
        st1["tsum"] = k.vec(lambda e, c=mi: e.tensor_tensor(
            stats[:, c:c + 1], stats[:, c:c + 1], stats[:, 8:9], ALU.add),
            waits=[t_red])
        st1["last"][bsel] = st1["tsum"]

    def gamma_chunk():
        t_pred = k.gps(lambda e: e.tensor_reduce(stats[0:1, 16:24], stats[:, 0:8],
                                                 AX.C, ALU.add), waits=[st1["tsum"]])
        t_arw = k.dma("gpsimd", ar_in, stats[0:1, 16:24], waits=[t_pred])
        t_ar = k.emit("gpsimd", lambda e: e.collective_compute(
            "AllReduce", ALU.add, replica_groups=RG8, ins=[ar_in], outs=[ar_out]),
            waits=[t_arw], sig="scc")
        t_gam0 = k.dma("sync", stats[0:1, 24:32], ar_out, waits=[t_ar])
        t_gg = t_gam0
        for i2, (nm, _, _, _) in enumerate(prep_mats):
            t_gg = k.vec(lambda e, i2=i2, nm=nm: e.tensor_scalar(
                stats[0:1, 32 + i2:33 + i2], stats[0:1, 24 + i2:25 + i2],
                1.0 / counts[nm], 1e-5, ALU.mult, ALU.max), waits=[t_gam0, t_gg])
        t_gam1 = k.vec(lambda e: e.reciprocal(stats[0:1, 40:45], stats[0:1, 32:37]),
                       waits=[t_gg])
        k.vec(lambda e: e.tensor_scalar(stats[0:1, 43:45], stats[0:1, 43:45],
                                        1.0 / 1.4, None, ALU.mult), waits=[t_gam1])
        t_gamb0 = k.tens(lambda e: e.matmul(pden[:, 0:16], onesf[0:1, :],
                                            stats[0:1, 32:48], start=True, stop=True),
                         waits=[k.tok("sv"), t_ones, pden_tok[0]])
        t_gamb = k.vec(lambda e: e.tensor_copy(gamb[:], pden[:, 0:16]), waits=[t_gamb0])
        pden_tok[0] = t_gamb
        st1["gamb"] = t_gamb
        t_cmb = k.vec(lambda e: e.tensor_tensor(stats[:, 120:121], gamb[:, 3:4],
                                                gamb[:, 3:4], ALU.mult), waits=[t_gamb])
        k.vec(lambda e: e.tensor_tensor(stats[:, 120:121], stats[:, 120:121],
                                        gamb[:, 4:5], ALU.mult), waits=[t_cmb])

    def quant_tile(ap_, ti, ntile, w_, mi, dest, dest_base, kind, tbufs_, qtmp_,
                   qbf_, tstage_):
        bsel = st1["nb"] % 2
        st1["nb"] += 1
        tld = k.dma("gpsimd", tbufs_[bsel][:, 0:w_], ap_[ti * 128:(ti + 1) * 128, :],
                    waits=[st1["last"][bsel]])
        k.vec(lambda e, b=tbufs_[bsel], w=w_, m=mi: e.tensor_scalar(
            qtmp_[:, 0:w], b[:, 0:w], gamb[:, 8 + m:9 + m], MAGIC, ALU.mult,
            ALU.add), waits=[tld, st1["gamb"], k.tok("st")])
        st1["last"][bsel] = k.tok("sv")
        k.vec(lambda e, w=w_: e.tensor_scalar(
            qtmp_[:, 0:w], qtmp_[:, 0:w], -MAGIC, 1.0, ALU.add, ALU.min))
        t3 = k.vec(lambda e, w=w_: e.tensor_scalar(
            qbf_[:, 0:w], qtmp_[:, 0:w], -1.0, None, ALU.max))
        nblk = w_ // 128
        for g0 in range(0, nblk, 8):
            gn = min(8, nblk - g0)
            ttr = None
            for j in range(gn):
                ttr = k.tens(lambda e, g=g0, j=j: e.transpose(
                    ptr[:, j * 128:(j + 1) * 128],
                    qbf_[:, (g + j) * 128:(g + j + 1) * 128], ident[:]),
                    waits=[t3, t_id, ptr_tok[0] if j == 0 else None,
                           k.tok("qa") if j == 0 else None],
                    sig=(j == gn - 1))
            tev = k.vec(lambda e, gn=gn, ts=tstage_: e.tensor_copy(
                ts[:, 0:gn * 128], ptr[:, 0:gn * 128]), waits=[ttr])
            ptr_tok[0] = tev
            for j in range(gn):
                kcol = g0 + j
                row = (dest_base + kcol * ntile + ti) if kind == "row" else \
                      (dest_base + kcol * 16 + ti)
                k.dma("scalar", dest[row], tstage_[:, j * 128:(j + 1) * 128], waits=[tev])

    bg1 = []
    for mat, src in ((0, wq_s), (1, wk_s), (2, wv_s)):
        for ti in range(8):
            bg1.append(lambda m=mat, s=src, t=ti: local_T_tile(
                s, t, lambda kc, tt, m=m: wqkvT_scr[m, kc, tt], tb_ln, tst_ln))
    bg2 = []
    for ti in range(16):
        bg2.append(lambda t=ti, f=(ti == 0): local_T_tile(
            wo_full, t, lambda kc, tt: woT_scr[kc, tt], tb_at, tst_at, first=f))
    bg2.append(lambda: k.dma("gpsimd", wqkv_bf, wqkv_s))
    bg2.append(lambda: k.dma("gpsimd", wout1_bf, wout1))
    for mi, (name, ap_, ntile, w_) in enumerate(prep_mats):
        for ti in range(ntile):
            bg2.append(lambda a=ap_, t=ti, w=w_, m=mi: pass1_tile(a, t, w, m, tb_at))
    bg2.append(gamma_chunk)
    quant_plan = [(gate_w_s, 8, 2048, 0, ag1_in, 0, "row"),
                  (val_w_s, 8, 2048, 1, ag1_in, 128, "row"),
                  (out_w_s, 16, 1024, 2, ag1_in, 256, "col"),
                  (w_fc_s, 5, 2048, 3, ag2_in, 0, "row"),
                  (w_proj_s, 16, 640, 4, ag2_in, 80, "col")]
    for ap_, ntile, w_, mi, dest, dbase, kind in quant_plan:
        for ti in range(ntile):
            bg2.append(lambda a=ap_, t=ti, n=ntile, w=w_, m=mi, d=dest, db=dbase,
                       kd=kind: quant_tile(a, t, n, w, m, d, db, kd, tb_at, qtmp_at,
                                           qbf_at, tst_at))

    def mk_drain(lst):
        q_ = list(lst)
        pos = [0]

        def drain(n=10 ** 9):
            e_ = min(pos[0] + n, len(q_))
            while pos[0] < e_:
                q_[pos[0]]()
                pos[0] += 1
        return drain

    drain1 = mk_drain(bg1)
    drain2 = mk_drain(bg2)

    lncnt = [0]

    def layernorm_tile(xin, lnout_bf, waits, junk, quant_col=None, lnf32=None):
        c = 32 + (lncnt[0] % 2) * 16
        lncnt[0] += 1
        st_ = stats
        t1 = k.vec(lambda e: e.tensor_reduce(st_[:, c:c + 1], xin, AX.X, ALU.add), waits=waits)
        t2 = k.vec(lambda e: e.scalar_tensor_tensor(
            junk, xin, 1.0, xin, ALU.mult, ALU.mult, accum_out=st_[:, c + 1:c + 2]),
            waits=waits)
        t3 = k.vec(lambda e: e.tensor_scalar(st_[:, c + 2:c + 3], st_[:, c:c + 1],
                                             1.0 / D, None, ALU.mult), waits=[t1])
        t4 = k.vec(lambda e: e.tensor_scalar(st_[:, c + 3:c + 4], st_[:, c + 1:c + 2],
                                             1.0 / D, None, ALU.mult), waits=[t2])
        t5 = k.vec(lambda e: e.tensor_tensor(st_[:, c + 4:c + 5], st_[:, c + 2:c + 3],
                                             st_[:, c + 2:c + 3], ALU.mult), waits=[t3])
        t6 = k.vec(lambda e: e.tensor_sub(st_[:, c + 5:c + 6], st_[:, c + 3:c + 4],
                                          st_[:, c + 4:c + 5]), waits=[t4, t5])
        t7 = k.scal(lambda e: e.activation(st_[:, c + 6:c + 7], st_[:, c + 5:c + 6], AF.Sqrt,
                                           bias=stats[:, 127:128], scale=1.0),
                    waits=[t6, t_eps])
        t8 = k.vec(lambda e: e.reciprocal(st_[:, c + 7:c + 8], st_[:, c + 6:c + 7]), waits=[t7])
        t9 = k.vec(lambda e: e.tensor_tensor(st_[:, c + 8:c + 9], st_[:, c + 2:c + 3],
                                             st_[:, c + 7:c + 8], ALU.mult), waits=[t8])
        t10 = k.vec(lambda e: e.tensor_scalar(st_[:, c + 9:c + 10], st_[:, c + 8:c + 9],
                                              -1.0, None, ALU.mult), waits=[t9])
        if quant_col is None:
            return k.vec(lambda e: e.tensor_scalar(lnout_bf, xin, st_[:, c + 7:c + 8],
                                                   st_[:, c + 9:c + 10], ALU.mult, ALU.add),
                         waits=[t10])
        ta = k.vec(lambda e: e.tensor_scalar(lnf32, xin, st_[:, c + 7:c + 8],
                                             st_[:, c + 9:c + 10], ALU.mult, ALU.add),
                   waits=[t10])
        tb = k.vec(lambda e: e.tensor_reduce(st_[:, c + 10:c + 11], lnf32, AX.X, ALU.max,
                                             apply_absolute_value=True), waits=[ta])
        tc = k.vec(lambda e: e.tensor_scalar(st_[:, c + 10:c + 11], st_[:, c + 10:c + 11],
                                             1e-5, None, ALU.max), waits=[tb])
        td = k.vec(lambda e: e.tensor_scalar(stats[:, quant_col:quant_col + 1],
                                             st_[:, c + 10:c + 11], 1.0 / 127.0, None,
                                             ALU.mult), waits=[tc])
        t_e = k.vec(lambda e: e.reciprocal(st_[:, c + 11:c + 12], st_[:, c + 10:c + 11]),
                    waits=[tc])
        tf = k.vec(lambda e: e.tensor_scalar(st_[:, c + 11:c + 12], st_[:, c + 11:c + 12],
                                             127.0, None, ALU.mult), waits=[t_e])
        tg = k.vec(lambda e: e.tensor_scalar(lnf32, lnf32, st_[:, c + 11:c + 12], MAGIC,
                                             ALU.mult, ALU.add), waits=[tf])
        return k.vec(lambda e: e.tensor_scalar(lnout_bf, lnf32, -MAGIC, None, ALU.add),
                     waits=[tg])

    def transpose_to_xt(src_bf, ttile, waits):
        last = None
        for g0 in (0, 8):
            ttr = None
            for j in range(8):
                ttr = k.tens(lambda e, g=g0, j=j, s=src_bf: e.transpose(
                    ptr[:, j * 128:(j + 1) * 128], s[:, (g + j) * 128:(g + j + 1) * 128],
                    ident[:]),
                    waits=list(waits) + [t_id, ptr_tok[0] if j == 0 else None],
                    sig=(j == 7))
            last = k.vec(lambda e, g=g0, tt=ttile: e.tensor_copy(
                xt[:, tt, g:g + 8, :].rearrange("p k x -> p (k x)"), ptr[:, 0:1024]),
                waits=[ttr])
            ptr_tok[0] = last
        return last

    def out_proj(lhs_view, rhs_rows_fn, bias_row, scale_ap, gate_toks):
        hbw = hb[:, 0:8192].rearrange("p (a b x) -> p a b x", a=4, b=16)
        rtok = None
        for kg in range(4):
            rr = rhs_rows_fn(kg)
            trh = None
            for a_ in range(4):
                trh = k.dma("sync", hbw[:, a_], rr[:, a_],
                            waits=([rtok] + list(gate_toks)) if a_ == 0 else [])
            for t in range(NTT):
                tm = None
                for kidx in range(4):
                    kc = kg * 4 + kidx
                    for n in range(4):
                        is_last = (kidx == 3) and not (kg == 3 and bias_row is not None)
                        tm = k.tens(lambda e, b=n, kc=kc, ki=kidx, t=t, n=n, il=is_last:
                                    e.matmul(pb[b][:],
                                             lhs_view[:, kc, t * 128:(t + 1) * 128],
                                             hbw[:, ki, n * 4:(n + 1) * 4, :],
                                             start=(ki == 0), stop=il),
                                    waits=[trh, bank_tok[n] if kidx == 0 else None]
                                    + list(gate_toks),
                                    sig=(kidx == 3 and n == 3 and
                                         not (kg == 3 and bias_row is not None)))
                if kg == 3 and bias_row is not None:
                    for n in range(4):
                        tm = k.tens(lambda e, b=n, n=n: e.matmul(
                            pb[b][:], onesb[:],
                            bias_row[:, n * 512:(n + 1) * 512],
                            start=False, stop=True), waits=[t_onesb], sig=(n == 3))
                for n in range(4):
                    if scale_ap is None:
                        bank_tok[n] = k.vec(lambda e, b=n, t=t, n=n: e.tensor_tensor(
                            res[:, t, n * 512:(n + 1) * 512],
                            res[:, t, n * 512:(n + 1) * 512], pb[b][:], ALU.add),
                            waits=[tm])
                    else:
                        bank_tok[n] = k.vec(lambda e, b=n, t=t, n=n: e.scalar_tensor_tensor(
                            res[:, t, n * 512:(n + 1) * 512], pb[b][:], scale_ap,
                            res[:, t, n * 512:(n + 1) * 512], ALU.mult, ALU.add),
                            waits=[tm])
            rtok = k.tok("st")

    ln_x = hbf32[:, 0:2048]
    lnbf_a = hb[:, 4096:6144]
    junk_a = p2f32[:, 6144:8192]

    t_prev_apply = None
    for i in range(16):
        tld = k.dma("sync", ln_x, x_seq[i * 128:(i + 1) * 128, :],
                    waits=[t_prev_apply, k.tok("qs") if i == 0 else None])
        t_prev_apply = layernorm_tile(ln_x, lnbf_a, [tld], junk_a)
        if debug_taps and i == 0:
            k.dma("gpsimd", taps["st"], stats[:], waits=[t_prev_apply])
            k.dma("gpsimd", taps["lnx"], ln_x, waits=[t_prev_apply])
        t_xt_all = transpose_to_xt(lnbf_a, i, [t_prev_apply])
        drain1(2)
    drain1()
    t_wT = k.tok("qa")

    if debug_taps:
        k.dma("gpsimd", taps["xt"], xt[:, :, :, :], waits=[k.tok("sv"), k.tok("st")])
    if stop_after == "ln0":
        return _finish(nc, k, stack)
    t_bq = k.dma("sync", stats[:, 64:72], bq_s.rearrange("(n p) -> p n", p=128),
                 allow_slow_non_contiguous=True)
    t_bk = k.dma("sync", stats[:, 72:80], bk_s.rearrange("(n p) -> p n", p=128),
                 allow_slow_non_contiguous=True)

    for mat, dst, bcol, tb in ((0, qf_scr, 64, t_bq), (1, kf_scr, 72, t_bk)):
        for n in range(HL):
            bsel = n % 2
            tw = k.dma("sync", wst[:, bsel],
                       wqkvT_scr[mat, :, n].rearrange("a p x -> p a x"),
                       waits=[wst_tok[bsel], tb, t_wT])
            for tc in range(4):
                bank = (n * 4 + tc) % 4
                tm = None
                for kc in range(NK):
                    tm = k.tens(lambda e, b=bank, bs=bsel, kc=kc, tc=tc: e.matmul(
                        pb[b][:], wst[:, bs, kc, :], xt[:, tc * 4:tc * 4 + 4, kc, :],
                        start=(kc == 0), stop=(kc == NK - 1)),
                        waits=[tw, t_xt_all, bank_tok[bank] if kc == 0 else None],
                        sig=(kc == NK - 1))
                ssel = (n * 4 + tc) % 2
                tev = k.vec(lambda e, b=bank, bc=bcol, n=n, ss=ssel: e.tensor_scalar(
                    stage[:, ss, :], pb[b][:], stats[:, bc + n:bc + n + 1], SCALE_QK,
                    ALU.add, ALU.mult), waits=[tm, stage_tok[ssel]])
                bank_tok[bank] = tev
                stage_tok[ssel] = k.dma(
                    "scalar", dst[n * 128:(n + 1) * 128, tc * 512:(tc + 1) * 512],
                    stage[:, ssel, :], waits=[tev])
            wst_tok[bsel] = k.tok("st")

    wv_full = pool2[:, 0:16384].rearrange("p (a b x) -> p a b x", a=16, b=8)
    bvrow = hb[0:1, 0:1024]
    t_brv = k.dma("gpsimd", bvrow, bv_s[None, :], waits=[k.tok("sv"), k.tok("ss")])
    t_wv = None
    for kc in range(NK):
        t_wv = k.dma("sync", wv_full[:, kc],
                     wqkvT_scr[2, kc].rearrange("b p x -> p b x"),
                     waits=[k.tok("sv"), t_wT] if kc == 0 else [])
    for t in range(16):
        for n2 in range(2):
            bank = 4 + ((t * 2 + n2) % 2)
            for kc in range(NK):
                k.tens(lambda e, b=bank, kc=kc, t=t, n2=n2: e.matmul(
                    pb[b][:], xt[:, t, kc, :], wv_full[:, kc, n2 * 4:(n2 + 1) * 4, :],
                    start=(kc == 0), stop=False),
                    waits=[t_wv, t_xt_all, bank_tok[bank] if kc == 0 else None], sig=False)
            tm = k.tens(lambda e, b=bank, n2=n2: e.matmul(
                pb[b][:], onesb[:], bvrow[:, n2 * 512:(n2 + 1) * 512],
                start=False, stop=True), waits=[t_onesb, t_brv], sig=True)
            ssel = (t * 2 + n2) % 2
            tev = k.vec(lambda e, b=bank, ss=ssel: e.tensor_copy(stage[:, ss, :], pb[b][:]),
                        waits=[tm, stage_tok[ssel]])
            bank_tok[bank] = tev
            stage_tok[ssel] = k.dma(
                "scalar", vt_scr[n2 * 4:(n2 + 1) * 4, t].rearrange("h p x -> p h x"),
                stage[:, ssel, :].rearrange("p (h x) -> p h x", h=4), waits=[tev])

    t_qkv_stored = k.tok("qa")
    if stop_after == "qkv0":
        return _finish(nc, k, stack)

    def qh_ap(b):
        return hb[:, 6144 * b + 0:6144 * b + 2048]

    def kh_ap(b):
        return hb[:, 6144 * b + 2048:6144 * b + 4096]

    def vh_ap(b):
        return hb[:, 6144 * b + 4096:6144 * b + 6144].rearrange("p (a x) -> p a x", a=16)

    def attn_scores(qsrc, ksrc, vsrc, ot_dst_fn, last_vec_gate, bg_cb=None):
        hload_tok = [None, None]
        pt_tok = [None, None]
        t_dmm_prev = None
        for h in range(HL):
            if bg_cb is not None:
                bg_cb()
            hs = h % 2
            base_waits = [hload_tok[hs], t_qkv_stored]
            if h < 2:
                base_waits.append(last_vec_gate)
            tq = k.dma("sync", qh_ap(hs), qsrc[h * 128:(h + 1) * 128, :], waits=base_waits)
            tk = k.dma("sync", kh_ap(hs), ksrc[h * 128:(h + 1) * 128, :],
                       waits=[hload_tok[hs]])
            tv = k.dma("sync", vh_ap(hs), vsrc[h].rearrange("a p x -> p a x"),
                       waits=[hload_tok[hs]])
            for qt in range(4):
                P = 4 * (qt + 1)
                av_bank = 4 + (qt % 2)
                t_av = None
                t_sumacc = None
                for kc in range(P):
                    sbank = kc % 2
                    tm = k.tens(lambda e, sb=sbank, kc=kc, qt=qt, hs=hs: e.matmul(
                        pb[sb][:], kh_ap(hs)[:, kc * 128:(kc + 1) * 128],
                        qh_ap(hs)[:, qt * 512:(qt + 1) * 512], start=True, stop=True),
                        waits=[tq, tk, bank_tok[sbank]])
                    texp = k.scal(lambda e, sb=sbank, kc=kc: e.activation(
                        pt[:, kc % 2, :], pb[sb][:], AF.Exp), waits=[tm, pt_tok[kc % 2]])
                    bank_tok[sbank] = texp
                    tmask = texp
                    off = kc * 128 - qt * 512
                    if off > -128:
                        mo = 384 - off
                        tmask = k.vec(lambda e, kc=kc, mo=mo: e.tensor_tensor(
                            pt[:, kc % 2, :], pt[:, kc % 2, :], smask[:, mo:mo + 512],
                            ALU.mult), waits=[texp])
                    if kc == 0:
                        t_sumacc = k.vec(lambda e, kc=kc: e.tensor_scalar(
                            sumacc[:], pt[:, kc % 2, :], 0.0, None, ALU.add),
                            waits=[tmask, t_dmm_prev])
                    else:
                        t_sumacc = k.vec(lambda e, kc=kc: e.tensor_tensor(
                            sumacc[:], sumacc[:], pt[:, kc % 2, :], ALU.add),
                            waits=[tmask, t_sumacc])
                    t_av = k.tens(lambda e, b=av_bank, kc=kc, P=P, hs=hs: e.matmul(
                        pb[b][:], vh_ap(hs)[:, kc, :], pt[:, kc % 2, :],
                        start=(kc == 0), stop=(kc == P - 1)),
                        waits=[tmask, tv, bank_tok[av_bank] if kc == 0 else None],
                        sig=(kc == P - 1))
                    pt_tok[kc % 2] = t_av
                t_dmm = k.tens(lambda e: e.matmul(pden[:], onesf[:], sumacc[:],
                                                  start=True, stop=True),
                               waits=[t_sumacc, t_ones, pden_tok[0]])
                t_dmm_prev = t_dmm
                t_rec = k.vec(lambda e: e.reciprocal(denr[:], pden[:]), waits=[t_dmm])
                pden_tok[0] = t_rec
                ssel = qt % 2
                tev = k.vec(lambda e, b=av_bank, ss=ssel: e.tensor_tensor(
                    stage[:, ss, :], pb[b][:], denr[:], ALU.mult),
                    waits=[t_av, t_rec, stage_tok[ssel]])
                bank_tok[av_bank] = tev
                stage_tok[ssel] = k.dma("scalar", ot_dst_fn(h, qt), stage[:, ssel, :],
                                        waits=[tev])
            hload_tok[hs] = k.tok("st")

    def ot_dst0(h, qt):
        return agot0_in.rearrange("(h d two) c -> h d two c", h=HL, two=2)[
            h, :, qt // 2, (qt % 2) * 512:(qt % 2) * 512 + 512]

    attn_scores(qf_scr, kf_scr, vt_scr, ot_dst0, k.tok("sv"),
                bg_cb=lambda: drain2(14))
    drain2()
    t_woT = k.tok("qa")

    t_ag0 = k.emit("gpsimd", lambda e: e.collective_compute(
        "AllGather", ALU.bypass, replica_groups=RG8, ins=[agot0_in], outs=[agot0_out]),
        waits=[k.tok("qa")], sig="scc")
    t_ag1 = k.emit("gpsimd", lambda e: e.collective_compute(
        "AllGather", ALU.bypass, replica_groups=RG8, ins=[ag1_in], outs=[ag1_out]),
        waits=[k.tok("qa")], sig="scc")
    t_ag2 = k.emit("gpsimd", lambda e: e.collective_compute(
        "AllGather", ALU.bypass, replica_groups=RG8, ins=[ag2_in], outs=[ag2_out]),
        waits=[k.tok("qa")], sig="scc")

    otg = pool2[:, 0:16384].rearrange("p (a x) -> p a x", a=16)
    gat_last = None
    for j in range(16):
        gat_last = k.emit("gpsimd", lambda e, j=j: e.indirect_dma_start(
            otg[:, j, :], None, agot0_out,
            bass.IndirectOffsetOnAxis(ap=gidx[:, j:j + 1], axis=0)),
            waits=[t_ag0, k.tok("st"), t_gidxl], sig="qg", inc=16)

    if debug_taps:
        k.dma("gpsimd", taps["otg"], otg, waits=[gat_last])
        k.dma("gpsimd", taps["qf"], qf_scr, waits=[])
        k.dma("gpsimd", taps["kf"], kf_scr, waits=[])
        k.dma("gpsimd", taps["vt"], vt_scr, waits=[])
        k.dma("gpsimd", taps["ot0"], agot0_in, waits=[])
    borow = hb[0:1, 8192:10240]
    t_bor = k.dma("gpsimd", borow, bo_full[None, :], waits=[k.tok("st"), k.tok("ss"),
                                                            k.tok("sv")])
    out_proj(otg,
             lambda kg: woT_scr[kg * 4:(kg + 1) * 4].rearrange("a b p x -> p a b x"),
             borow, None, [gat_last, t_bor, t_woT])

    if debug_taps:
        for i in range(NTT):
            k.dma("scalar", taps["a0"][i * 128:(i + 1) * 128, :], res[:, i, :],
                  waits=[k.tok("sv")])

    k.barrier()
    lnf_f0 = hbf32[:, 0:2048]
    junk_f0 = hbf32[:, 2048:4096]
    xqbf_f0 = hb[:, 8192:10240]
    for t in range(NTT):
        tq = layernorm_tile(res[:, t, :], xqbf_f0, [k.tok("sv")], junk_f0,
                            quant_col=16 + t, lnf32=lnf_f0)
        transpose_to_xt(xqbf_f0, t, [tq])
    t_gsxa = k.vec(lambda e: e.tensor_scalar(stats[:, 80:88], stats[:, 16:24], gamb[:, 0:1],
                                             None, ALU.mult), waits=[k.tok("sv")])
    t_gsx = k.vec(lambda e: e.tensor_scalar(stats[:, 88:96], stats[:, 16:24], gamb[:, 1:2],
                                            None, ALU.mult), waits=[t_gsxa])
    t_rmz = k.vec(lambda e: e.memset(stats[:, 24:32], 0.0))
    rmax_tok = [None]

    f0w = pool2[:, 0:16384].rearrange("p (m a l x) -> p m a l x", m=2, a=16, l=4)
    ag1v = ag1_out.rearrange("(r a) p x -> r a p x", r=8)
    zg_f = hbf32[:, 0:512]
    g_f = hbf32[:, 512:1024]
    zv_f = hbf32[:, 1024:1536]
    t_f0w = None
    gb_tok = [None, None]
    for j in range(16):
        r, l0 = j // 2, 4 * (j % 2)
        jw = [k.tok("st"), t_ag1] if j == 0 else [k.tok("st")]
        for kc in range(NK):
            t_f0w = k.dma("sync", f0w[:, 0, kc],
                          ag1v[r, kc * 8 + l0:kc * 8 + l0 + 4].rearrange("l p x -> p l x"),
                          waits=(jw + [t_f0w]) if kc == 0 else [])
            k.dma("sync", f0w[:, 1, kc],
                  ag1v[r, 128 + kc * 8 + l0:128 + kc * 8 + l0 + 4]
                  .rearrange("l p x -> p l x"), waits=[])
        gsel = 0
        tgb = k.dma("sync", gbb[:, gsel, 0, :],
                    gate_b[j * 512:(j + 1) * 512][None, :].broadcast_to([128, 512]),
                    waits=[gb_tok[0], k.tok("sv") if j else None])
        k.dma("sync", gbb[:, gsel, 1, :],
              val_b[j * 512:(j + 1) * 512][None, :].broadcast_to([128, 512]), waits=[])
        t_f0w_now = k.tok("qs")
        for t in range(NTT):
            bg = (t % 2) * 2
            bv_ = bg + 1
            tm = None
            for kc in range(NK):
                k.tens(lambda e, b=bg, kc=kc, t=t: e.matmul(
                    pb[b][:], xt[:, t, kc, :], f0w[:, 0, kc, :, :],
                    start=(kc == 0), stop=(kc == NK - 1)),
                    waits=[t_f0w_now, k.tok("sv") if (j == 0 and t == 0) else None,
                           bank_tok[bg] if kc == 0 else None], sig=False)
                tm = k.tens(lambda e, b=bv_, kc=kc, t=t: e.matmul(
                    pb[b][:], xt[:, t, kc, :], f0w[:, 1, kc, :, :],
                    start=(kc == 0), stop=(kc == NK - 1)),
                    waits=[bank_tok[bv_] if kc == 0 else None],
                    sig=(kc == NK - 1))
            t_zg = k.vec(lambda e, b=bg, t=t, gs=gsel: e.scalar_tensor_tensor(
                zg_f, pb[b][:], stats[:, 80 + t:81 + t], gbb[:, gs, 0, :],
                ALU.mult, ALU.add), waits=[tm, tgb, t_gsx])
            t_sil = k.scal(lambda e: e.activation(g_f, zg_f, AF.Silu), waits=[t_zg])
            t_zv = k.vec(lambda e, b=bv_, t=t, gs=gsel: e.scalar_tensor_tensor(
                zv_f, pb[b][:], stats[:, 88 + t:89 + t], gbb[:, gs, 1, :],
                ALU.mult, ALU.add), waits=[tm, t_gsx])
            bank_tok[bg] = t_zg
            bank_tok[bv_] = t_zv
            ssel = t % 2
            t_gv = k.vec(lambda e, ss=ssel: e.tensor_tensor(
                stage[:, ss, :], g_f, zv_f, ALU.mult),
                waits=[t_sil, t_zv, stage_tok[ssel]])
            t_rr = k.vec(lambda e, ss=ssel: e.tensor_reduce(
                stats[:, 9:10], stage[:, ss, :], AX.X,
                ALU.max, apply_absolute_value=True),
                waits=[t_gv, t_rmz, rmax_tok[0]])
            rmax_tok[0] = k.vec(lambda e, t=t: e.tensor_tensor(
                stats[:, 24 + t:25 + t], stats[:, 24 + t:25 + t],
                stats[:, 9:10], ALU.max), waits=[t_rr])
            stage_tok[ssel] = k.dma("scalar", gv_scr[t, :, j * 512:(j + 1) * 512],
                                    stage[:, ssel, :], waits=[t_gv])
        gb_tok[0] = k.tok("sv")

    t_s2a = k.vec(lambda e: e.tensor_scalar(stats[:, 24:32], stats[:, 24:32], 1e-5, None,
                                            ALU.max), waits=[rmax_tok[0]])
    t_s2b = k.vec(lambda e: e.tensor_scalar(stats[:, 104:112], stats[:, 24:32], 1.0 / 127.0,
                                            None, ALU.mult), waits=[t_s2a])
    t_osx0 = k.vec(lambda e: e.tensor_scalar(stats[:, 112:120], stats[:, 104:112],
                                             gamb[:, 2:3], None, ALU.mult), waits=[t_s2b])
    t_rec8 = k.vec(lambda e: e.reciprocal(stats[:, 96:104], stats[:, 24:32]), waits=[t_s2a])
    t_s2 = k.vec(lambda e: e.tensor_scalar(stats[:, 96:104], stats[:, 96:104], 127.0,
                                           None, ALU.mult), waits=[t_rec8])

    outw = pool2[:, 0:16384].rearrange("p (a b x) -> p a b x", a=8, b=16)
    gvin = [hb[:, 0:1024], hb[:, 1024:2048]]
    gvtmp = hbf32[:, 1024:2048]
    qbf = hb[:, 4096:5120]
    qslab = hb[:, 5120:6144].rearrange("p (a x) -> p a x", a=8)
    t_outw = None
    gvin_tok = [None, None]
    for hg in range(8):
        for kl in range(8):
            t_outw = k.dma("sync", outw[:, kl],
                           ag1v[hg, 256 + kl * 16:256 + (kl + 1) * 16]
                           .rearrange("b p x -> p b x"),
                           waits=[t_outw, k.tok("st"),
                                  t_ag1 if hg == 0 else None] if kl == 0 else [])
        for t in range(NTT):
            gsel = t % 2
            tgv = k.dma("sync", gvin[gsel], gv_scr[t, :, hg * 1024:(hg + 1) * 1024],
                        waits=[gvin_tok[gsel], k.tok("qa") if (hg == 0 and t < 2) else None])
            tq1 = k.vec(lambda e, gs=gsel, t=t: e.tensor_scalar(
                gvtmp, gvin[gs], stats[:, 96 + t:97 + t], MAGIC, ALU.mult, ALU.add),
                waits=[tgv, t_s2])
            tq2 = k.vec(lambda e: e.tensor_scalar(qbf, gvtmp, -MAGIC, None, ALU.add))
            gvin_tok[gsel] = tq2
            ttr = None
            for jj in range(8):
                ttr = k.tens(lambda e, jj=jj: e.transpose(
                    ptr[:, jj * 128:(jj + 1) * 128], qbf[:, jj * 128:(jj + 1) * 128],
                    ident[:]), waits=[tq2, t_id, ptr_tok[0] if jj == 0 else None],
                    sig=(jj == 7))
            tsl = k.vec(lambda e: e.tensor_copy(
                qslab.rearrange("p a x -> p (a x)"), ptr[:, 0:1024]),
                waits=[ttr, k.tok("st")])
            ptr_tok[0] = tsl
            tm = None
            for kl in range(8):
                for n in range(4):
                    tm = k.tens(lambda e, b=n, kl=kl, n=n: e.matmul(
                        pb[b][:], qslab[:, kl, :], outw[:, kl, n * 4:(n + 1) * 4, :],
                        start=(kl == 0), stop=(kl == 7)),
                        waits=[tsl, t_outw, bank_tok[n] if kl == 0 else None],
                        sig=(kl == 7 and n == 3))
            for n in range(4):
                bank_tok[n] = k.vec(lambda e, b=n, t=t, n=n: e.scalar_tensor_tensor(
                    res[:, t, n * 512:(n + 1) * 512], pb[b][:], stats[:, 112 + t:113 + t],
                    res[:, t, n * 512:(n + 1) * 512], ALU.mult, ALU.add),
                    waits=[tm, t_osx0])
    for n in range(4):
        tb = k.dma("sync", gbb[:, 0, 0, :],
                   out_b[n * 512:(n + 1) * 512][None, :].broadcast_to([128, 512]),
                   waits=[gb_tok[0], k.tok("sv")])
        for t in range(NTT):
            gb_tok[0] = k.vec(lambda e, t=t, n=n: e.tensor_tensor(
                res[:, t, n * 512:(n + 1) * 512], res[:, t, n * 512:(n + 1) * 512],
                gbb[:, 0, 0, :], ALU.add), waits=[tb])

    if debug_taps:
        for i in range(NTT):
            k.dma("scalar", taps["f0"][i * 128:(i + 1) * 128, :], res[:, i, :],
                  waits=[k.tok("sv")])

    k.barrier()
    lnbf_a1 = hb[:, 8192:10240]
    junk_a1 = p2f32[:, 6144:8192]
    ln_stage = hb[:, 0:1024]
    agxv = agx_in.rearrange("(a p) c -> a p c", p=128)
    for t in range(NTT):
        tq = layernorm_tile(res[:, t, :], lnbf_a1, [k.tok("sv")], junk_a1)
        for g0 in (0, 8):
            ttr = None
            for j in range(8):
                ttr = k.tens(lambda e, g=g0, j=j: e.transpose(
                    ptr[:, j * 128:(j + 1) * 128],
                    lnbf_a1[:, (g + j) * 128:(g + j + 1) * 128], ident[:]),
                    waits=[tq, t_id, ptr_tok[0] if j == 0 else None,
                           k.tok("qa") if j == 0 else None], sig=(j == 7))
            tev = k.vec(lambda e: e.tensor_copy(ln_stage, ptr[:, 0:1024]), waits=[ttr])
            ptr_tok[0] = tev
            k.dma("scalar",
                  agxv[g0:g0 + 8, :, t * 128:(t + 1) * 128].rearrange("a p x -> p a x"),
                  ln_stage.rearrange("p (a x) -> p a x", a=8), waits=[tev])
    t_agx = k.emit("gpsimd", lambda e: e.collective_compute(
        "AllGather", ALU.bypass, replica_groups=RG8, ins=[agx_in], outs=[agx_out]),
        waits=[k.tok("qa")], sig="scc")
    gat_stage = [hb[:, 0:1024], hb[:, 1024:2048]]
    gstok = [None, None]
    for half in range(2):
        for kc in range(NK):
            gsel = (half * NK + kc) % 2
            col = 16 + half * 16 + kc
            tg = k.emit("gpsimd", lambda e, gs=gsel, col=col: e.indirect_dma_start(
                gat_stage[gs], None, agx_out,
                bass.IndirectOffsetOnAxis(ap=gidx[:, col:col + 1], axis=0)),
                waits=[t_agx, gstok[gsel], t_gidxl, k.tok("qa")], sig="qg", inc=16)
            gstok[gsel] = k.vec(lambda e, gs=gsel, half=half, kc=kc: e.tensor_copy(
                xt[:, half * 8:half * 8 + 8, kc, :],
                gat_stage[gs].rearrange("p (a x) -> p a x", a=8)), waits=[tg])
    t_xt1 = k.tok("sv")

    kvbuf = p2f32[:, 0:2048]
    scanA = p2f32[:, 2048:4096]
    scanB = p2f32[:, 4096:6144]
    qph = hb[:, 0:2048]
    kph = hb[:, 2048:4096]
    vph = hb[:, 4096:6144]
    e1f = hbf32[:, 3072:3584]
    e2f = hbf32[:, 3584:4096]
    wqbf = wqkv_bf.rearrange("(a p) c -> a p c", p=128)

    def ot_dst1(h, tc):
        return agot1_in.rearrange("(h d two) c -> h d two c", h=HL, two=2)[
            h, :, tc // 2, (tc % 2) * 512:(tc % 2) * 512 + 512]

    t_headv = None
    t_dmm_last = None
    for hl in range(HL):
        for part, dstbuf in ((0, qph), (1, kph), (2, vph)):
            chunk = part * 8 + hl
            bsel = (hl * 3 + part) % 2
            tw = k.dma("sync", wst[:, bsel],
                       wqbf[:, :, chunk * 128:(chunk + 1) * 128].rearrange(
                           "a p x -> p a x"), waits=[wst_tok[bsel]])
            for tc in range(4):
                bank = tc % 4
                tm = None
                for kc in range(NK):
                    tm = k.tens(lambda e, b=bank, bs=bsel, kc=kc, tc=tc: e.matmul(
                        pb[b][:], wst[:, bs, kc, :], xt[:, tc * 4:tc * 4 + 4, kc, :],
                        start=(kc == 0), stop=(kc == NK - 1)),
                        waits=[tw, t_xt1, bank_tok[bank] if kc == 0 else None],
                        sig=(kc == NK - 1))
                if part < 2:
                    tmin = k.vec(lambda e, b=bank: e.tensor_scalar(
                        e1f, pb[b][:], 0.0, None, ALU.min), waits=[tm])
                    texp = k.scal(lambda e: e.activation(e2f, e1f, AF.Exp), waits=[tmin])
                    tphi = k.vec(lambda e, b=bank, d=dstbuf, tc=tc: e.scalar_tensor_tensor(
                        d[:, tc * 512:(tc + 1) * 512], pb[b][:], 0.0, e2f,
                        ALU.max, ALU.add), waits=[texp])
                    bank_tok[bank] = tphi
                else:
                    bank_tok[bank] = k.vec(lambda e, b=bank, d=dstbuf, tc=tc: e.tensor_scalar(
                        d[:, tc * 512:(tc + 1) * 512], pb[b][:], 0.0, None, ALU.add),
                        waits=[tm])
            wst_tok[bsel] = k.tok("st")
        t_qkv_h = k.tok("sv")
        dslice = decb[:, hl:hl + 1].broadcast_to([128, S])
        t_kv = k.vec(lambda e: e.tensor_tensor(kvbuf, kph, vph, ALU.mult),
                     waits=[t_qkv_h, t_headv, t_dmm_last])
        t_sa = k.vec(lambda e: e.tensor_tensor_scan(scanA, dslice, kvbuf, 0.0,
                                                    ALU.mult, ALU.add),
                     waits=[t_kv, t_dec2])
        t_sb = k.vec(lambda e: e.tensor_tensor_scan(scanB, dslice, kph, 0.0,
                                                    ALU.mult, ALU.add))
        t_qs = k.vec(lambda e: e.tensor_tensor(kvbuf, qph, scanB, ALU.mult), waits=[t_sb])
        for tc in range(4):
            t_dmm = k.tens(lambda e, tc=tc: e.matmul(
                pden[:], onesf[:], kvbuf[:, tc * 512:(tc + 1) * 512], start=True, stop=True),
                waits=[t_qs, t_ones, pden_tok[0]])
            t_dmm_last = t_dmm
            t_cl = k.vec(lambda e: e.tensor_scalar(denr[:], pden[:], 1e-6, None, ALU.max),
                         waits=[t_dmm])
            pden_tok[0] = t_cl
            t_rc = k.vec(lambda e: e.reciprocal(denr[:], denr[:]))
            t_o1 = k.vec(lambda e, tc=tc: e.tensor_tensor(
                e1f, qph[:, tc * 512:(tc + 1) * 512], scanA[:, tc * 512:(tc + 1) * 512],
                ALU.mult), waits=[t_sa])
            ssel = tc % 2
            t_o2 = k.vec(lambda e, ss=ssel: e.tensor_tensor(
                stage[:, ss, :], e1f, denr[:], ALU.mult),
                waits=[t_o1, t_rc, stage_tok[ssel]])
            stage_tok[ssel] = k.dma("scalar", ot_dst1(hl, tc), stage[:, ssel, :],
                                    waits=[t_o2])
        t_headv = k.tok("sv")

    t_ag1o = k.emit("gpsimd", lambda e: e.collective_compute(
        "AllGather", ALU.bypass, replica_groups=RG8, ins=[agot1_in], outs=[agot1_out]),
        waits=[k.tok("qa")], sig="scc")
    gat_last1 = None
    for j in range(16):
        gat_last1 = k.emit("gpsimd", lambda e, j=j: e.indirect_dma_start(
            otg[:, j, :], None, agot1_out,
            bass.IndirectOffsetOnAxis(ap=gidx[:, j:j + 1], axis=0)),
            waits=[t_ag1o, k.tok("sv"), t_gidxl], sig="qg", inc=16)

    wo1bf = wout1_bf.rearrange("(a p) c -> a p c", p=128)
    out_proj(otg,
             lambda kg: wo1bf[kg * 4:(kg + 1) * 4].rearrange("a p (b x) -> p a b x", x=128),
             None, None, [gat_last1])

    if debug_taps:
        for i in range(NTT):
            k.dma("scalar", taps["a1"][i * 128:(i + 1) * 128, :], res[:, i, :],
                  waits=[k.tok("sv")])

    k.barrier()
    lnbf_f1 = hb[:, 8192:10240]
    junk_f1 = p2f32[:, 6144:8192]
    relu_f = hbf32[:, 0:512]
    ag2v = ag2_out.rearrange("(r a) p x -> r a p x", r=8)
    projw = pool2[:, 0:16384].rearrange("p (a x) -> p a x", a=8)

    def _hsq_base(hc, th):
        return 16384 + hc * 512 if hc < 32 else (1 - th) * 8192 + (hc - 32) * 512

    def hsq_ap(hc, th, t4):
        base = _hsq_base(hc, th)
        return xtflat[:, base + t4 * 128: base + t4 * 128 + 128]

    def hsq_chunk_ap(hc, th):
        base = _hsq_base(hc, th)
        return xtflat[:, base: base + 512]

    t_lastev = None

    for th in range(2):
        for t4 in range(4):
            t = th * 4 + t4
            tq = layernorm_tile(res[:, t, :], lnbf_f1, [k.tok("sv")], junk_f1)
            transpose_to_xt(lnbf_f1, t, [tq])
        t_xtf1 = k.tok("sv")
        for hc in range(40):
            bsel = hc % 2
            r, l = (hc // 5), (hc % 5)
            tw = k.dma("sync", wst[:, bsel],
                       ag2v[r, 0:80].rearrange("(a l) p x -> a l p x", l=5)[:, l]
                       .rearrange("a p x -> p a x"),
                       waits=[wst_tok[bsel], t_ag2 if hc == 0 else None])
            bank = hc % 2
            tm = None
            for kc in range(NK):
                tm = k.tens(lambda e, b=bank, bs=bsel, kc=kc, th=th: e.matmul(
                    pb[b][:], wst[:, bs, kc, :], xt[:, th * 4:th * 4 + 4, kc, :],
                    start=(kc == 0), stop=(kc == NK - 1)),
                    waits=[tw, t_xtf1, bank_tok[bank] if kc == 0 else None],
                    sig=(kc == NK - 1))
            trl = k.scal(lambda e, b=bank: e.activation(relu_f, pb[b][:], AF.Relu),
                         waits=[tm, t_lastev])
            tlast = k.vec(lambda e, b=bank, hc=hc, th=th: e.tensor_tensor(
                hsq_chunk_ap(hc, th), relu_f, pb[b][:], ALU.mult), waits=[trl])
            t_lastev = tlast
            bank_tok[bank] = tlast
            wst_tok[bsel] = k.tok("st")
        t_pw = None
        for hg in range(5):
            for kl in range(8):
                kG = hg * 8 + kl
                r2, l2 = kG // 5, kG % 5
                t_pw = k.dma("sync", projw[:, kl, :].rearrange("p (b x) -> p b x", b=16),
                             ag2v[r2, 80 + l2 * 16:80 + (l2 + 1) * 16]
                             .rearrange("b p x -> p b x"),
                             waits=[t_pw if kl == 0 and hg > 0 else None,
                                    k.tok("st") if kl == 0 else None])
            t_pwd = k.tok("qs")
            tphsq = k.tok("sv")
            for t4 in range(4):
                t = th * 4 + t4
                tm = None
                for kl in range(8):
                    hc = hg * 8 + kl
                    for n in range(4):
                        tm = k.tens(lambda e, b=n, hc=hc, th=th, t4=t4, kl=kl, n=n: e.matmul(
                            pb[b][:], hsq_ap(hc, th, t4),
                            projw[:, kl, n * 512:(n + 1) * 512],
                            start=(kl == 0), stop=(kl == 7)),
                            waits=[t_pwd, tphsq, bank_tok[n] if kl == 0 else None],
                            sig=(kl == 7 and n == 3))
                for n in range(4):
                    bank_tok[n] = k.vec(lambda e, b=n, t=t, n=n: e.scalar_tensor_tensor(
                        res[:, t, n * 512:(n + 1) * 512], pb[b][:], stats[:, 120:121],
                        res[:, t, n * 512:(n + 1) * 512], ALU.mult, ALU.add), waits=[tm])

    for i in range(NTT):
        k.dma("scalar", y_out[i * 128:(i + 1) * 128, :], res[:, i, :],
              waits=[k.tok("sv")])
    tfin = k.tok("qa")
    k.emit("gpsimd", lambda e: e.nop(), waits=[tfin])
    k.emit("sync", lambda e: e.nop(), waits=[tfin])

    return _finish(nc, k, stack)


def _finish(nc, k, stack):
    tfin2 = [k.tok("qa"), k.tok("qs"), k.tok("qg")]
    k.emit("gpsimd", lambda e: e.nop(), waits=tfin2)
    k.emit("sync", lambda e: e.nop(), waits=tfin2)
    with nc.Block() as block:
        @block.sync
        def _(e):
            for op in k.ops["sync"]:
                op(e)

        @block.scalar
        def _(e):
            for op in k.ops["scalar"]:
                op(e)

        @block.vector
        def _(e):
            for op in k.ops["vector"]:
                op(e)

        @block.tensor
        def _(e):
            for op in k.ops["tensor"]:
                op(e)

        @block.gpsimd
        def _(e):
            for op in k.ops["gpsimd"]:
                op(e)

    stack.close()
    n_ops = sum(len(v) for v in k.ops.values())
    return nc, n_ops


_CACHE = {}


def _get_nc():
    if "nc" not in _CACHE:
        _CACHE["nc"] = build(False)[0]
    return _CACHE["nc"]


def _mk_inmap(c, I):
    b, h = c // 2, c % 2
    hb0 = h * HL
    sl = slice(hb0 * 128, (hb0 + HL) * 128)
    wqkv = I["L1_attn_w_qkv"]
    x = np.ascontiguousarray
    bf = lambda a: x(np.asarray(a, np.float32)).astype(ml_dtypes.bfloat16)

    cc = np.arange(896)[None, :] - np.arange(128)[:, None]
    smask_np = (cc >= 384).astype(ml_dtypes.bfloat16)

    d_ = np.arange(128, dtype=np.uint32)
    gi = np.zeros((128, 48), np.uint32)
    for s in range(2):
        for hl in range(HL):
            rank = 2 * (c // 2) + s
            gi[:, s * 8 + hl] = rank * 2048 + hl * 256 + d_ * 2 + (c % 2)
    for half in range(2):
        for kc in range(NK):
            rank = 2 * (c // 2) + half
            gi[:, 16 + half * 16 + kc] = rank * 2048 + kc * 128 + d_

    return {
        "x_own": x(I["L0_attn_X"][b, h * 1024:(h + 1) * 1024]),
        "x_seq": x(I["L0_attn_X"][b]),
        "wq_s": x(I["L0_attn_Wq"][sl]),
        "wk_s": x(I["L0_attn_Wk"][sl]),
        "wv_s": x(I["L0_attn_Wv"][sl]),
        "wo_full": x(I["L0_attn_Wo"]),
        "bq_s": x(I["L0_attn_bq"][sl]),
        "bk_s": x(I["L0_attn_bk"][sl]),
        "bv_s": x(I["L0_attn_bv"][sl]),
        "bo_full": x(I["L0_attn_bo"]),
        "gate_w_s": x(I["L0_ffn_gate_w"][c * 1024:(c + 1) * 1024]),
        "val_w_s": x(I["L0_ffn_val_w"][c * 1024:(c + 1) * 1024]),
        "out_w_s": x(I["L0_ffn_out_w"][:, c * 1024:(c + 1) * 1024]),
        "gate_b": x(I["L0_ffn_gate_b"]),
        "val_b": x(I["L0_ffn_val_b"]),
        "out_b": x(I["L0_ffn_out_b"]),
        "wqkv_s": x(np.concatenate([wqkv[:, sl],
                                    wqkv[:, 2048 + hb0 * 128:2048 + (hb0 + HL) * 128],
                                    wqkv[:, 4096 + hb0 * 128:4096 + (hb0 + HL) * 128]], 1)),
        "wout1": x(I["L1_attn_w_out"]),
        "decay_s": x(I["L1_attn_decay_param"][hb0:hb0 + HL].reshape(1, HL)),
        "w_fc_s": x(I["L1_ffn_w_fc"][c * 640:(c + 1) * 640]),
        "w_proj_s": x(I["L1_ffn_w_proj"][:, c * 640:(c + 1) * 640]),
        "smask_in": smask_np,
        "gidx_in": gi,
    }


def kernel(**inputs):
    nc = _get_nc()
    in_maps = [_mk_inmap(c, inputs) for c in range(8)]
    resu = run_bass_kernel_spmd(nc, in_maps, core_ids=list(range(8)))
    y = np.empty((B, S, D), np.float32)
    for c in range(8):
        b, h = c // 2, c % 2
        y[b, h * 1024:(h + 1) * 1024] = resu.results[c]["y"]
    return y
